# version 59
# baseline (speedup 1.0000x reference)
import contextlib
import numpy as np
import ml_dtypes
import concourse.bass as bass
import concourse.mybir as mybir
from concourse.bass_utils import run_bass_kernel_spmd

F32 = mybir.dt.float32
BF16 = mybir.dt.bfloat16
AF = mybir.ActivationFunctionType
ALU = mybir.AluOpType
AX = mybir.AxisListType

N_CORES = 8
B_PER_CORE = 2
SEQ = 2048
D = 1024
NT = B_PER_CORE * SEQ // 128
NTB = SEQ // 128
SHIFT = 12.0
CLIPF = float(np.float32((128.0 - 1e-5) / 128.0))

ENGS = ("sync", "gpsimd", "tensor", "vector", "scalar")


class Walk:

    def __init__(self):
        self.ops = {e: [] for e in ENGS}
        self.cnt = {"sd": 0, "sg": 0, "tt": 0, "vv": 0, "ss": 0, "sa": 0}
        self.lastw = {}

    def emit(self, eng, build, waits=(), inc=None, inc_amt=None):
        wl = []
        if eng == "vector" and self.cnt["vv"] > 0:
            wl.append(("vv", self.cnt["vv"]))
        for s, v in waits:
            if v is None:
                continue
            assert v <= self.cnt[s], f"wait ahead of emission: {s} {v} > {self.cnt[s]}"
            k = (eng, s)
            if v > self.lastw.get(k, 0):
                wl.append((s, v))
                self.lastw[k] = v
        ret = None
        if inc is not None:
            amt = inc_amt if inc_amt is not None else (16 if inc in ("sd", "sg", "sa") else 1)
            self.cnt[inc] += amt
            ret = self.cnt[inc]
        else:
            amt = 0

        def f(e, sems, wl=tuple(wl), build=build, inc=inc, amt=amt):
            for s, v in wl:
                e.wait_ge(sems[s], v)
            ins = build(e)
            if inc is not None:
                ins.then_inc(sems[inc], amt)

        self.ops[eng].append(f)
        return ret


def cdiv(a, b):
    return (a + b - 1) // b


def build_program():
    nc = bass.Bass("TRN2", target_bir_lowering=False, debug=False)

    xin = nc.dram_tensor("xin", [NT * 128, D], F32, kind="ExternalInput").ap()
    wkT = nc.dram_tensor("wkT", [D, D], F32, kind="ExternalInput").ap()
    wqT = nc.dram_tensor("wqT", [D, D], F32, kind="ExternalInput").ap()
    wvT = nc.dram_tensor("wvT", [D, D], F32, kind="ExternalInput").ap()
    woT = nc.dram_tensor("woT", [D, D], F32, kind="ExternalInput").ap()
    wffn = nc.dram_tensor("wffn", [D, D], F32, kind="ExternalInput").ap()
    gbc_d = nc.dram_tensor("gbc", [128, 16], F32, kind="ExternalInput").ap()
    wqd = nc.dram_tensor("wqd", [D, D], BF16, kind="Internal")
    identb = nc.dram_tensor("identb", [128, 128], BF16, kind="ExternalInput").ap()
    onesf = nc.dram_tensor("onesf", [128, 128], F32, kind="ExternalInput").ap()
    maskb = nc.dram_tensor("maskb", [128, 128], BF16, kind="ExternalInput").ap()
    cstd = nc.dram_tensor("cstd", [128, 8], F32, kind="ExternalInput").ap()
    zrow = nc.dram_tensor("zrow", [1, 512], BF16, kind="ExternalInput").ap()
    outp = nc.dram_tensor("out", [NT * 128, D], F32, kind="ExternalOutput").ap()
    x2d = nc.dram_tensor("x2d", [NT * 128, D], F32, kind="Internal")
    xnd = nc.dram_tensor("xnd", [NT * 128, D], BF16, kind="Internal")
    xn2d = nc.dram_tensor("xn2d", [NT * 128, D], BF16, kind="Internal")
    ccin = nc.dram_tensor("ccin", [128, 1], F32, kind="Internal")
    ccout = nc.dram_tensor("ccout", [128, 1], F32, kind="Internal")

    cstb = nc.alloc_sbuf_tensor("cstb", [128, 8], F32)
    zrob = nc.alloc_sbuf_tensor("zrob", [1, 512], BF16)
    idb = nc.alloc_sbuf_tensor("idb", [128, 128], BF16)
    onb = nc.alloc_sbuf_tensor("onb", [128, 128], F32)
    mkb = nc.alloc_sbuf_tensor("mkb", [128, 128], BF16)
    gbc = nc.alloc_sbuf_tensor("gbcs", [128, 16], F32)
    sgb = nc.alloc_sbuf_tensor("sgb", [128, 1024], BF16)
    sc = nc.alloc_sbuf_tensor("sc", [128, 16], F32)
    stat = nc.alloc_sbuf_tensor("stat", [128, 112], F32)
    rs = nc.alloc_sbuf_tensor("rs", [128, 48], F32)
    wbf = nc.alloc_sbuf_tensor("wbf", [128, 4 * 8192], BF16)
    bfA = nc.alloc_sbuf_tensor("bfA", [128, 32768], BF16)
    ktq = nc.alloc_sbuf_tensor("ktq", [128, 4096], BF16)
    vpb = nc.alloc_sbuf_tensor("vpb", [128, 256], BF16)
    eeb = nc.alloc_sbuf_tensor("eeb", [128, 8192], BF16)
    xnb = nc.alloc_sbuf_tensor("xnb", [128, 1024], BF16)
    xn2t = nc.alloc_sbuf_tensor("xn2t", [128, 2048], BF16)
    xst = nc.alloc_sbuf_tensor("xst", [128, 2048], F32)
    x2st = nc.alloc_sbuf_tensor("x2st", [128, 2048], F32)
    acc32 = nc.alloc_sbuf_tensor("acc32", [128, 1024], F32)
    res32 = nc.alloc_sbuf_tensor("res32", [128, 1024], F32)
    resB = nc.alloc_sbuf_tensor("resB", [128, 1024], F32)
    wstage = nc.alloc_sbuf_tensor("wstage", [128, 1024], F32)
    grow = nc.alloc_sbuf_tensor("grow", [1, 128], F32)

    psS = nc.alloc_psum_tensor("psS", [128, 2048], F32)
    psSl = [psS[:, i * 512:(i + 1) * 512] for i in range(4)]
    psZT = nc.alloc_psum_tensor("psZT", [128, 2048], F32)

    stk = contextlib.ExitStack()
    sems = {s: stk.enter_context(nc.semaphore(s)) for s in ("sd", "sg", "tt", "vv", "ss", "sa")}

    W = Walk()
    E = W.emit

    def mm(out, lhsT, rhs, start, stop, waits=(), tile_position=None):
        return E("tensor",
                 lambda t, o=out, l=lhsT, r=rhs, st_=start, sp=stop, tp=tile_position:
                 t.matmul(o, l, r, start=st_, stop=sp, tile_position=tp),
                 waits=waits, inc="tt")

    E("sync", lambda e: e.dma_start(out=cstb[:], in_=cstd), inc="sd")
    E("sync", lambda e: e.dma_start(out=onb[:], in_=onesf), inc="sd")
    E("sync", lambda e: e.dma_start(out=idb[:], in_=identb), inc="sd")
    E("sync", lambda e: e.dma_start(out=mkb[:], in_=maskb), inc="sd")
    E("sync", lambda e: e.dma_start(out=zrob[:], in_=zrow), inc="sd")
    E("sync", lambda e: e.dma_start(out=gbc[:], in_=gbc_d), inc="sd")
    consts_done = W.cnt["sd"]

    wsrc = [wkT, wqT, wvT, woT]
    wload = None
    for w in range(4):
        for j in range(8):
            wload = E("gpsimd",
                      lambda e, w=w, j=j: e.dma_start(
                          out=wbf[:, w * 8192 + j * 1024:w * 8192 + (j + 1) * 1024],
                          in_=wsrc[w][j * 128:(j + 1) * 128, :]),
                      inc="sg")

    p1 = {"wst_free": None, "red_prev": None, "k": 0}

    def emit_p1_strip():
        k = p1["k"]
        p1["k"] += 1
        ld = E("gpsimd", lambda e, k=k: e.dma_start(out=wstage[:],
                                                    in_=wffn[k * 128:(k + 1) * 128, :]),
               waits=[p1["wst_free"]] if p1["wst_free"] else [], inc="sg")
        aw = [("sg", ld), ("sd", consts_done)]
        if p1["red_prev"] is not None:
            aw.append(("vv", p1["red_prev"]))
        ab_s = E("scalar", lambda e: e.activation(acc32[:], wstage[:], AF.Abs),
                 waits=aw, inc="ss")
        p1["wst_free"] = ("ss", ab_s)
        p1["red_prev"] = E("vector", lambda e, k=k: e.reduce_sum(
            stat[:, 96 + k:97 + k], acc32[:], axis=AX.X), waits=[("ss", ab_s)],
            inc="vv")

    pp = {}

    def emit_beta_pass2():
        sumabs_v = E("vector", lambda e: e.reduce_sum(sc[:, 0:1], stat[:, 96:104],
                                                      axis=AX.X),
                     waits=[("vv", p1["red_prev"])], inc="vv")
        t1w = [("vv", sumabs_v), ("sd", consts_done)]
        if ps_guard[0] is not None:
            t1w.append(ps_guard[0])
        t1 = mm(psSl[0][0:1, 0:1], onb[:, 0:1], sc[:, 0:1], True, True, waits=t1w)
        v1 = E("vector", lambda e: e.tensor_copy(sc[0:1, 1:2], psSl[0][0:1, 0:1]),
               waits=[("tt", t1)], inc="vv")
        t2 = mm(psSl[0][:, 1:2], onb[0:1, 0:128], sc[0:1, 1:2], True, True,
                waits=[("vv", v1)])
        v2 = E("vector", lambda e: e.tensor_copy(sc[:, 2:3], psSl[0][:, 1:2]),
               waits=[("tt", t2)], inc="vv")
        ps_guard[0] = ("vv", v2)
        beta_v = E("vector", lambda e: e.tensor_scalar(
            sc[:, 3:4], sc[:, 2:3], float(2.0 ** -20), None, ALU.mult),
            waits=[("vv", v2)], inc="vv")
        thr_v = E("vector", lambda e: e.tensor_scalar(
            sc[:, 4:5], sc[:, 3:4], 0.5, 5e-6, ALU.mult, ALU.add),
            waits=[("vv", beta_v)], inc="vv")
        bcp_v = E("vector", lambda e: e.tensor_copy(sc[:, 5:6], sc[:, 3:4]),
                  waits=[("vv", beta_v)], inc="vv")
        bsq_v = E("vector", lambda e: e.tensor_tensor(
            sc[:, 6:7], sc[:, 3:4], sc[:, 5:6], ALU.mult),
            waits=[("vv", bcp_v)], inc="vv")
        pp.update(thr_v=thr_v, bsq_v=bsq_v, wst_free2=p1["wst_free"],
                  mk_prev=None, wq_store=None, k=0)

    def emit_p2_strip():
        k = pp["k"]
        pp["k"] += 1
        ld2 = E("gpsimd", lambda e, k=k: e.dma_start(
            out=wstage[:], in_=wffn[k * 128:(k + 1) * 128, :]),
            waits=[pp["wst_free2"]] if pp["wst_free2"] else [], inc="sg")
        aw = [("sg", ld2)]
        if pp["mk_prev"] is not None:
            aw.append(("vv", pp["mk_prev"]))
        ab2 = E("scalar", lambda e: e.activation(res32[:], wstage[:], AF.Abs),
                waits=aw, inc="ss")
        mkw = [("ss", ab2), ("vv", pp["thr_v"])]
        if pp["mk_prev"] is None and a1b1["st_last"] is not None:
            mkw.append(("sd", a1b1["st_last"]))
        mk = E("vector", lambda e: e.tensor_scalar(
            xnb[:], res32[:], sc[:, 4:5], None, ALU.is_gt),
            waits=mkw, inc="vv")
        pp["mk_prev"] = mk
        sg_ = E("scalar", lambda e: e.activation(sgb[:], wstage[:], AF.Sign),
                waits=[("sg", ld2)], inc="ss")
        pp["wst_free2"] = ("ss", sg_)
        ww = [("vv", mk), ("ss", sg_)]
        if pp["wq_store"] is not None:
            ww.append(pp["wq_store"])
        tt_ = E("vector", lambda e: e.tensor_tensor(
            xn2t[:, 1024:2048], xnb[:], sgb[:], ALU.mult), waits=ww,
            inc="vv")
        pp["wq_store"] = ("sd", E("sync", lambda e, k=k: e.dma_start(
            out=wqd.ap()[k * 128:(k + 1) * 128, :], in_=xn2t[:, 1024:2048]),
            waits=[("vv", tt_)], inc="sd"))

    ps_guard = {0: None, 1: None, 2: None, 3: None}

    prev_sqrt = {}
    prev_recip = {}

    def ln_bn(src0, src1, eps_col, base, w0, w1, mid=None, var_scale=None):
        E("vector", lambda e, base=base: e.bn_stats(
            stat[:, base:base + 6], src0), waits=w0, inc="vv")
        s1 = E("vector", lambda e, base=base: e.bn_stats(
            stat[:, base + 6:base + 12], src1), waits=w1, inc="vv")
        aw = [("vv", s1)]
        if base in prev_sqrt:
            aw.append(("ss", prev_sqrt[base]))
        ag = E("vector", lambda e, base=base: e.bn_aggr(
            stat[:, base + 12:base + 14], stat[:, base:base + 12]),
            waits=aw, inc="vv")
        sw = [("vv", ag), ("sd", consts_done)]
        ln_s = E("scalar", lambda e, base=base: e.activation(
            stat[:, base + 14:base + 15], stat[:, base + 13:base + 14],
            AF.Ln, bias=cstb[:, eps_col:eps_col + 1]), waits=sw, inc="ss")
        prev_sqrt[base] = ln_s
        if mid is not None:
            mid()
        rstd_s = E("scalar", lambda e, base=base: e.activation(
            stat[:, base + 15:base + 16], stat[:, base + 14:base + 15],
            AF.Exp, bias=cstb[:, 5:6], scale=cstb[:, 6:7]),
            waits=[("ss", ln_s)], inc="ss")
        prev_recip[base] = rstd_s
        return ("ss", rstd_s)

    prev_apply = {}

    def ln_apply_scalar(base, src_ap, out_ap, rstd_tok, extra_waits=()):
        mcol = 105 + base // 16
        mw = [rstd_tok]
        if base in prev_apply:
            mw.append(("ss", prev_apply[base]))
        mrs = E("vector", lambda e, base=base, mcol=mcol: e.tensor_scalar(
            stat[:, mcol:mcol + 1], stat[:, base + 12:base + 13], -1.0,
            stat[:, base + 15:base + 16], ALU.mult, ALU.mult),
            waits=mw, inc="vv")
        ap_s = E("scalar", lambda e, base=base, mcol=mcol: e.activation(
            out_ap, src_ap, AF.Identity, bias=stat[:, mcol:mcol + 1],
            scale=stat[:, base + 15:base + 16]),
            waits=[("vv", mrs)] + list(extra_waits), inc="ss")
        prev_apply[base] = ap_s
        return ap_s


    xst_free = [None, None]
    a4_r1_prev = None
    a4_r2_prev = None
    x2st_store_free = [None, None]
    x2st_vec_free = [None, None]
    xn2b_guard = None
    xn2w_last = None
    x2w_done = [None] * NT
    qkv_wk_last = None
    qkv_done_prev = None

    a1b1 = {"lt": 0, "ct": 0, "ld": {}, "xnb_guard": None, "st_last": None}

    def emit_a1b1_load():
        t = a1b1["lt"]
        a1b1["lt"] += 1
        par = t % 2
        gt = NTB + t
        xw = [xst_free[par]] if xst_free[par] else []
        a1b1["ld"][t] = E("sync", lambda e, gt=gt, par=par: e.dma_start(
            out=xst[:, par * 1024:(par + 1) * 1024],
            in_=xin[gt * 128:(gt + 1) * 128, :]), waits=xw, inc="sd")

    def emit_a1b1_compute():
        t = a1b1["ct"]
        a1b1["ct"] += 1
        par = t % 2
        gt = NTB + t
        base = 16 * par
        ld = a1b1["ld"][t]
        src = xst[:, par * 1024:(par + 1) * 1024]
        rstd_v = ln_bn(xst[:, par * 1024:par * 1024 + 512],
                       xst[:, par * 1024 + 512:(par + 1) * 1024],
                       2, base, [("sd", ld)], [("sd", ld)])
        xn_s = ln_apply_scalar(base, src, xnb[:], rstd_v,
                               [a1b1["xnb_guard"]] if a1b1["xnb_guard"] else [])
        xst_free[par] = ("ss", xn_s)
        st_ = E("sync", lambda e, gt=gt: e.dma_start(
            out=xnd.ap()[gt * 128:(gt + 1) * 128, :], in_=xnb[:]),
            waits=[("ss", xn_s)], inc="sd")
        a1b1["xnb_guard"] = ("sd", st_)
        a1b1["st_last"] = st_

    for b in range(B_PER_CORE):
        if b == 1:
            emit_beta_pass2()
        xnb_guard = None
        for t in (range(NTB) if b == 0 else []):
            gt = b * NTB + t
            par = t % 2
            base = 16 * par
            xw = [xst_free[par]] if xst_free[par] else []
            ld = E("sync", lambda e, gt=gt, par=par: e.dma_start(
                out=xst[:, par * 1024:(par + 1) * 1024],
                in_=xin[gt * 128:(gt + 1) * 128, :]), waits=xw, inc="sd")
            src = xst[:, par * 1024:(par + 1) * 1024]
            src0 = xst[:, par * 1024:par * 1024 + 512]
            src1 = xst[:, par * 1024 + 512:(par + 1) * 1024]
            rstd_v = ln_bn(src0, src1, 2, base, [("sd", ld)], [("sd", ld)])
            xn_s = ln_apply_scalar(base, src, xnb[:], rstd_v,
                                   [xnb_guard] if xnb_guard else [])
            xst_free[par] = ("ss", xn_s)
            st_ = E("sync", lambda e, gt=gt: e.dma_start(
                out=xnd.ap()[gt * 128:(gt + 1) * 128, :], in_=xnb[:]),
                waits=[("ss", xn_s)], inc="sd")
            xnb_guard = ("sd", st_)
        trl = None
        for j in range(8):
            trw = [xnb_guard if b == 0 else ("sd", a1b1["st_last"])]
            if b > 0 and qkv_done_prev is not None:
                trw.append(("tt", qkv_done_prev))
            trl = E("sync", lambda e, j=j, b=b: e.dma_start(
                out=bfA[:, j * 2048:(j + 1) * 2048],
                in_=xnd.ap()[b * SEQ:(b + 1) * SEQ, j * 128:(j + 1) * 128],
                transpose=True), waits=trw, inc="sd")
        xnT_ready = ("sd", trl)
        if b == 0:
            emit_a1b1_load()
            emit_a1b1_load()

        zT_evac_v = None
        ktq_guard = None
        for p in range(8):
            kt_last_evac = None
            rot = 0
            for which, woff, koff in ((0, 0, 0), (1, 8192, 2048)):
                for cb in range(4):
                    psi = rot % 4
                    rot += 1
                    ps = psSl[psi]
                    gw = [("sg", wload), xnT_ready]
                    if ktq_guard is not None:
                        gw.append(ktq_guard)
                    if ps_guard[psi] is not None:
                        gw.append(ps_guard[psi])
                    last = None
                    col0 = cb * 512
                    for j in range(8):
                        wts = gw if j == 0 else ()
                        last = mm(ps[:],
                                  wbf[:, woff + j * 1024 + p * 128:
                                      woff + j * 1024 + (p + 1) * 128],
                                  bfA[:, j * 2048 + col0: j * 2048 + col0 + 512],
                                  j == 0, j == 7, waits=wts)
                    ev = E("vector", lambda e, ps=ps, koff=koff, cb=cb: e.tensor_copy(
                        ktq[:, koff + cb * 512: koff + (cb + 1) * 512], ps[:]),
                        waits=[("tt", last)], inc="vv")
                    ps_guard[psi] = ("vv", ev)
                    kt_last_evac = ev
            if p == 7:
                qkv_wk_last = W.cnt["tt"]

            kq_ready = ("vv", kt_last_evac)
            zt_anti = ("vv", zT_evac_v) if zT_evac_v is not None else None
            exp_done = {}
            z_tok = {}
            vp_tok = {}
            rs_guard = {}
            zero_done = {}
            last_stop_mm = None

            def emit_z(i):
                nonlocal last_stop_mm
                par = i % 2
                ncols = (i + 1) * 128
                for q in range(cdiv(ncols, 512)):
                    wq_ = min(512, ncols - q * 512)
                    if q not in zero_done:
                        zw = [("vv", vp_tok[i])]
                        if zt_anti is not None:
                            zw.append(zt_anti)
                        zero_done[q] = mm(psZT[:, q * 512:q * 512 + wq_],
                                          zrob[0:1, 0:128], zrob[0:1, 0:wq_],
                                          True, False, waits=zw)
                    for h in range(2):
                        stop = (i == NTB - 1) and (h == 1)
                        zw = [("vv", vp_tok[i]), ("ss", exp_done[(h, i, q)])]
                        last_stop_mm = mm(
                            psZT[h * 64:(h + 1) * 64, q * 512:q * 512 + wq_],
                            vpb[:, par * 128 + h * 64: par * 128 + (h + 1) * 64],
                            eeb[:, par * 4096 + h * 2048 + q * 512:
                                par * 4096 + h * 2048 + q * 512 + wq_],
                            False, stop, waits=zw,
                            tile_position=(0, h * 64))
                z_tok[i] = last_stop_mm

            rot2 = 0
            for i in range(NTB):
                par = i % 2
                ncols = (i + 1) * 128
                nb5 = cdiv(ncols, 512)
                group_cnt = {}
                for h in range(2):
                    prow = slice(h * 64, h * 64 + 64)
                    blocks = []
                    for b5 in range(nb5):
                        psi = rot2 % 4
                        rot2 += 1
                        pS = psSl[psi]
                        wblk = min(512, ncols - b5 * 512)
                        wts = [kq_ready]
                        if ps_guard[psi] is not None:
                            wts.append(ps_guard[psi])
                        diag = b5 == nb5 - 1
                        last = mm(pS[:, 0:wblk],
                                  ktq[prow, i * 128:(i + 1) * 128],
                                  ktq[prow, 2048 + b5 * 512:
                                      2048 + b5 * 512 + wblk],
                                  True, not diag, waits=wts)
                        if diag:
                            last = mm(pS[:, wblk - 128:wblk],
                                      idb[:], mkb[:], False, True,
                                      waits=[("sd", consts_done)])
                        blocks.append((psi, last, wblk))
                    b5 = 0
                    ng = 0
                    while b5 < nb5:
                        psi = blocks[b5][0]
                        if b5 + 1 < nb5 and psi != 3:
                            width = blocks[b5][2] + blocks[b5 + 1][2]
                            last = blocks[b5 + 1][1]
                            nblk = 2
                        else:
                            width = blocks[b5][2]
                            last = blocks[b5][1]
                            nblk = 1
                        acol = par * 24 + h * 4 + ng
                        eoff = par * 4096 + h * 2048 + b5 * 512
                        poff = psi * 512
                        ew = [("tt", last)]
                        if (i - 2) in z_tok:
                            ew.append(("tt", z_tok[i - 2]))
                        if rs_guard.get((par, h)) is not None:
                            ew.append(("vv", rs_guard[(par, h)]))
                        tok = E("scalar",
                                lambda e, poff=poff, width=width, eoff=eoff,
                                acol=acol: e.activation(
                                    eeb[:, eoff:eoff + width],
                                    psS[:, poff:poff + width], AF.Exp,
                                    bias=cstb[:, 1:2], scale=cstb[:, 0:1],
                                    accum_out=rs[:, acol:acol + 1]),
                                waits=ew, inc="ss")
                        for k in range(nblk):
                            exp_done[(h, i, b5 + k)] = tok
                            ps_guard[blocks[b5 + k][0]] = ("ss", tok)
                        b5 += nblk
                        ng += 1
                    group_cnt[h] = ng
                psi = rot2 % 4
                rot2 += 1
                psV = psSl[psi]
                gv = [("sg", wload), xnT_ready]
                if ps_guard[psi] is not None:
                    gv.append(ps_guard[psi])
                lastv = None
                for j in range(8):
                    wts = gv if j == 0 else ()
                    lastv = mm(psV[:, 0:128],
                               bfA[:, j * 2048 + i * 128: j * 2048 + (i + 1) * 128],
                               wbf[:, 2 * 8192 + j * 1024 + p * 128:
                                   2 * 8192 + j * 1024 + (p + 1) * 128],
                               j == 0, j == 7, waits=wts)
                vp_v = None
                for h in range(2):
                    pc = par * 24
                    ng = group_cnt[h]
                    last_tok = exp_done[(h, i, nb5 - 1)]
                    if ng == 1:
                        tot = E("vector", lambda e, h=h, pc=pc: e.reciprocal(
                            rs[:, pc + 12 + h:pc + 13 + h],
                            rs[:, pc + h * 4:pc + h * 4 + 1]),
                            waits=[("ss", last_tok)], inc="vv")
                    else:
                        ca, cb_ = pc + 8 + h, pc + 10 + h
                        s_ = E("vector", lambda e, h=h, pc=pc, ca=ca: e.tensor_add(
                            rs[:, ca:ca + 1], rs[:, pc + h * 4:pc + h * 4 + 1],
                            rs[:, pc + h * 4 + 1:pc + h * 4 + 2]),
                            waits=[("ss", last_tok)], inc="vv")
                        cur = ca
                        for g in range(2, ng):
                            nxt = cb_ if cur == ca else ca
                            s_ = E("vector",
                                   lambda e, h=h, pc=pc, g=g, cur=cur, nxt=nxt:
                                   e.tensor_add(
                                       rs[:, nxt:nxt + 1], rs[:, cur:cur + 1],
                                       rs[:, pc + h * 4 + g:pc + h * 4 + g + 1]),
                                   waits=[("ss", last_tok)], inc="vv")
                            cur = nxt
                        tot = E("vector", lambda e, h=h, pc=pc, cur=cur: e.reciprocal(
                            rs[:, pc + 12 + h:pc + 13 + h], rs[:, cur:cur + 1]),
                            waits=[("vv", s_)], inc="vv")
                    rs_guard[(par, h)] = tot
                    vw2 = [("vv", tot), ("tt", lastv)]
                    if (i - 2) in z_tok:
                        vw2.append(("tt", z_tok[i - 2]))
                    vp_v = E("vector", lambda e, h=h, par=par, pc=pc, psV=psV:
                             e.tensor_scalar(
                                 vpb[:, par * 128 + h * 64: par * 128 + (h + 1) * 64],
                                 psV[:, h * 64:(h + 1) * 64],
                                 rs[:, pc + 12 + h:pc + 13 + h], None, ALU.mult),
                             waits=vw2, inc="vv")
                vp_tok[i] = vp_v
                ps_guard[psi] = ("vv", vp_v)
                if i > 0:
                    emit_z(i - 1)
            emit_z(NTB - 1)
            zT_evac_v = E("vector", lambda e, p=p: e.tensor_copy(
                bfA[:, 16384 + p * 2048: 16384 + (p + 1) * 2048], psZT[:]),
                waits=[("tt", last_stop_mm)], inc="vv")
            ktq_guard = ("tt", last_stop_mm)
            if b == 0:
                emit_p1_strip()
                emit_a1b1_compute()
                emit_a1b1_compute()
                if p < 7:
                    emit_a1b1_load()
                    emit_a1b1_load()
            else:
                emit_p2_strip()
        qkv_done_prev = W.cnt["tt"]

        a4_ld = {}

        def a4_load(t):
            par = t % 2
            gt_ = b * NTB + t
            xw = [xst_free[par]] if xst_free[par] else []
            a4_ld[t] = E("gpsimd", lambda e, gt_=gt_, par=par: e.dma_start(
                out=xst[:, par * 1024:(par + 1) * 1024],
                in_=xin[gt_ * 128:(gt_ + 1) * 128, :]), waits=xw, inc="sg")

        a4_mm = {}

        def a4_mms_evacs(t):
            nonlocal a4_r1_prev
            lasts = []
            for eb in range(2):
                psi = (2 * t + eb) % 4
                ps = psSl[psi]
                gw = [("vv", zT_evac_v), ("sg", wload)]
                if ps_guard[psi] is not None:
                    gw.append(ps_guard[psi])
                last = None
                for j in range(8):
                    wts = gw if j == 0 else ()
                    last = mm(ps[:],
                              bfA[:, 16384 + j * 2048 + t * 128:
                                  16384 + j * 2048 + (t + 1) * 128],
                              wbf[:, 3 * 8192 + j * 1024 + eb * 512:
                                  3 * 8192 + j * 1024 + (eb + 1) * 512],
                              j == 0, j == 7, waits=wts)
                lasts.append((last, psi, ps))
            evw = [("tt", lasts[0][0])]
            if a4_r1_prev is not None:
                evw.append(a4_r1_prev)
            ev0 = E("scalar", lambda e, ps=lasts[0][2]: e.activation(
                acc32[:, 0:512], ps[:], AF.Copy), waits=evw, inc="ss")
            ev = E("scalar", lambda e, ps=lasts[1][2]: e.activation(
                acc32[:, 512:1024], ps[:], AF.Copy),
                waits=[("tt", lasts[1][0])], inc="ss")
            ps_guard[lasts[0][1]] = ("ss", ev0)
            ps_guard[lasts[1][1]] = ("ss", ev)
            a4_mm[t] = (ev0, ev)

        a4_load(0)
        a4_load(1)
        a4_mms_evacs(0)
        for t in range(NTB):
            gt = b * NTB + t
            par = t % 2
            base = 32 + 16 * par
            base2 = 64 + 16 * par
            ev0, ev = a4_mm[t]
            rstd_v = ln_bn(acc32[:, 0:512], acc32[:, 512:1024], 2, base,
                           [("ss", ev0)], [("ss", ev)])
            r1 = ln_apply_scalar(base, acc32[:], resB[:], rstd_v,
                                 [a4_r2_prev] if a4_r2_prev else [])
            a4_r1_prev = ("ss", r1)
            r2w = [("ss", r1), ("sg", a4_ld[t])]
            if x2st_store_free[par] is not None:
                r2w.append(x2st_store_free[par])
            if x2st_vec_free[par] is not None:
                r2w.append(x2st_vec_free[par])
            r2 = E("vector", lambda e, par=par: e.tensor_tensor(
                x2st[:, par * 1024:(par + 1) * 1024], resB[:],
                xst[:, par * 1024:(par + 1) * 1024], ALU.add),
                waits=r2w, inc="vv")
            xst_free[par] = ("vv", r2)
            a4_r2_prev = ("vv", r2)
            if t + 2 < NTB:
                a4_load(t + 2)
            if t + 1 < NTB:
                a4_mms_evacs(t + 1)
            st_eng, st_sem = ("sync", "sd") if b == 0 else ("gpsimd", "sg")
            x2w_done[gt] = (st_sem, E(st_eng, lambda e, gt=gt, par=par: e.dma_start(
                out=x2d.ap()[gt * 128:(gt + 1) * 128, :],
                in_=x2st[:, par * 1024:(par + 1) * 1024]),
                waits=[("vv", r2)], inc=st_sem))
            x2st_store_free[par] = x2w_done[gt]
            rstd_b1 = ln_bn(x2st[:, par * 1024:par * 1024 + 512],
                            x2st[:, par * 1024 + 512:(par + 1) * 1024],
                            3, base2, [("vv", r2)], [("vv", r2)])
            xc = ln_apply_scalar(base2, x2st[:, par * 1024:(par + 1) * 1024],
                                 xn2t[:, 0:1024], rstd_b1,
                                 [xn2b_guard] if xn2b_guard else [])
            x2st_vec_free[par] = ("ss", xc)
            xn2w_last = E("gpsimd", lambda e, gt=gt: e.dma_start(
                out=xn2d.ap()[gt * 128:(gt + 1) * 128, :], in_=xn2t[:, 0:1024]),
                waits=[("ss", xc)], inc="sg")
            xn2b_guard = ("sg", xn2w_last)

        if b == 0:
            xn2w_b0_last = xn2w_last

    oproj_tt = W.cnt["tt"]

    wq_ready = None
    for k in range(8):
        wq_ready = E("gpsimd", lambda e, k=k: e.dma_start(
            out=wbf[:, k * 1024:(k + 1) * 1024],
            in_=wqd.ap()[k * 128:(k + 1) * 128, :]),
            waits=[pp["wq_store"], ("tt", qkv_wk_last)], inc="sg")
    wq_ready = ("sg", wq_ready)

    STG = 8192
    u_free = {}
    gmax_v = None
    gcnt = 0
    for idx, (h, j) in enumerate([(h, j) for h in range(2) for j in range(8)]):
        slot = idx % 4
        tw = [("tt", qkv_wk_last)]
        if h == 0:
            tw.append(("sg", xn2w_b0_last))
        else:
            tw.append(("sg", xn2w_last))
        if slot in u_free:
            tw.append(u_free[slot])
        trl = E("sync", lambda e, j=j, h=h, slot=slot: e.dma_start(
            out=wbf[:, STG + slot * 2048:STG + (slot + 1) * 2048],
            in_=xn2d.ap()[h * 2048:(h + 1) * 2048, j * 128:(j + 1) * 128],
            transpose=True), waits=tw, inc="sd")
        dst = j * 4096 + h * 2048
        uw = [("sd", trl)]
        if dst < 16384:
            uw.append(("tt", qkv_done_prev))
        else:
            uw.append(("tt", oproj_tt))
        u = E("vector", lambda e, j=j, slot=slot, dst=dst: e.tensor_scalar(
            bfA[:, dst:dst + 2048], wbf[:, STG + slot * 2048:STG + (slot + 1) * 2048],
            gbc[:, j:j + 1], gbc[:, 8 + j:9 + j], ALU.mult, ALU.add),
            waits=uw, inc="vv")
        u_free[slot] = ("vv", u)
        mxw = E("vector", lambda e, dst=dst: e.reduce_max(
            stat[:, 104:105], bfA[:, dst:dst + 2048], axis=AX.X,
            apply_absolute_value=True), waits=[("vv", u)], inc="vv")
        grd = 8 + ((gcnt - 1) % 2)
        gwr = 8 + (gcnt % 2)
        if gcnt == 0:
            gmax_v = E("vector", lambda e: e.tensor_copy(sc[:, 8:9],
                                                         stat[:, 104:105]),
                       waits=[("vv", mxw)], inc="vv")
        else:
            gmax_v = E("vector", lambda e, grd=grd, gwr=gwr: e.tensor_tensor(
                sc[:, gwr:gwr + 1], sc[:, grd:grd + 1], stat[:, 104:105],
                ALU.max), waits=[("vv", mxw)], inc="vv")
        gcnt += 1
    apply_done = ("vv", gmax_v)

    gcol = 8 + ((32 - 1) % 2)
    g1 = E("gpsimd", lambda e, gcol=gcol: e.dma_start(out=ccin.ap(),
                                                      in_=sc[:, gcol:gcol + 1]),
           waits=[("vv", gmax_v)], inc="sg")
    g2 = E("gpsimd", lambda e: e.collective_compute(
        "AllReduce", ALU.max, replica_groups=[list(range(N_CORES))],
        ins=[ccin.ap()], outs=[ccout.ap()]), waits=[("sg", g1)], inc="sg", inc_amt=1)
    g3 = E("gpsimd", lambda e: e.dma_start(out=grow[:],
                                           in_=ccout.ap().rearrange("p o -> o p")),
           waits=[("sg", g2)], inc="sg")
    vg = E("vector", lambda e: e.reduce_max(sc[0:1, 10:11], grow[0:1, :], axis=AX.X),
           waits=[("sg", g3)], inc="vv")
    tbw = [("vv", vg)]
    if ps_guard[0] is not None:
        tbw.append(ps_guard[0])
    tb = mm(psSl[0][:, 1:2], onb[0:1, 0:128], sc[0:1, 10:11], True, True, waits=tbw)
    vb = E("vector", lambda e: e.tensor_copy(sc[:, 11:12], psSl[0][:, 1:2]),
           waits=[("tt", tb)], inc="vv")
    ps_guard[0] = ("vv", vb)
    E("vector", lambda e: e.tensor_scalar(sc[:, 12:13], sc[:, 11:12], CLIPF, None,
                                          ALU.mult), waits=[("vv", vb)], inc="vv")
    tclip_v = E("vector", lambda e: e.tensor_scalar(sc[:, 13:14], sc[:, 11:12], -CLIPF,
                                                    None, ALU.mult), inc="vv")

    res_guard = None
    out_done = None
    b3_r1_prev = None
    b3_r2_prev = None
    xn2t_guard = [None, None]
    clip_done = {}
    ld_done = {}

    bfA_j = bfA[:, 0:32768].rearrange("p (j c) -> p j c", j=8)

    def emit_clips(t):
        par = t % 2
        cw = [("vv", tclip_v), apply_done]
        if xn2t_guard[par] is not None:
            cw.append(xn2t_guard[par])
        xt = E("vector", lambda e, t=t, par=par: e.tensor_scalar(
            xn2t[:, par * 1024:(par + 1) * 1024],
            bfA_j[:, :, t * 128:(t + 1) * 128],
            sc[:, 12:13], sc[:, 13:14],
            ALU.min, ALU.max), waits=cw, inc="vv")
        clip_done[t] = ("vv", xt)

    def emit_x2_load(t):
        par = t % 2
        lw = [x2w_done[t]]
        if x2st_store_free[par] is not None:
            lw.append(x2st_store_free[par])
        if x2st_vec_free[par] is not None:
            lw.append(x2st_vec_free[par])
        ld_done[t] = E("gpsimd", lambda e, t=t, par=par: e.dma_start(
            out=x2st[:, par * 1024:(par + 1) * 1024],
            in_=x2d.ap()[t * 128:(t + 1) * 128, :]), waits=lw, inc="sg")

    emit_clips(0)
    emit_clips(1)
    emit_x2_load(0)
    emit_x2_load(1)
    ps_guard2 = {0: None, 1: None, 2: None, 3: None}
    for t in range(NT):
        par = t % 2
        base = 32 + 16 * par
        lasts = []
        for ub in range(2):
            psi = (2 * t + ub) % 4
            ps = psSl[psi]
            gw = [clip_done[t], wq_ready]
            if ps_guard[psi] is not None:
                gw.append(ps_guard[psi])
            if ps_guard2[psi] is not None:
                gw.append(ps_guard2[psi])
            last = None
            for j in range(8):
                wts = gw if j == 0 else ()
                last = mm(ps[:],
                          xn2t[:, par * 1024 + j * 128: par * 1024 + (j + 1) * 128],
                          wbf[:, j * 1024 + ub * 512: j * 1024 + (ub + 1) * 512],
                          j == 0, j == 7, waits=wts)
            lasts.append((last, psi, ps))
        xn2t_guard[par] = ("tt", lasts[1][0])
        yw0 = [("tt", lasts[0][0]), ("vv", pp["thr_v"])]
        if b3_r1_prev is not None:
            yw0.append(b3_r1_prev)
        yv0 = E("scalar", lambda e, ps=lasts[0][2]: e.activation(
            acc32[:, 0:512], ps[:], AF.Copy, scale=sc[:, 3:4]), waits=yw0, inc="ss")
        yv = E("scalar", lambda e, ps=lasts[1][2]: e.activation(
            acc32[:, 512:1024], ps[:], AF.Copy, scale=sc[:, 3:4]),
            waits=[("tt", lasts[1][0])], inc="ss")
        ps_guard[lasts[0][1]] = ("ss", yv0)
        ps_guard[lasts[1][1]] = ("ss", yv)
        rstd_v = ln_bn(acc32[:, 0:512], acc32[:, 512:1024], 2, base,
                       [("ss", yv0)], [("ss", yv)],
                       mid=(lambda t=t: emit_clips(t + 2)) if t + 2 < NT else None)
        r1 = ln_apply_scalar(base, acc32[:], resB[:], rstd_v,
                             [b3_r2_prev] if b3_r2_prev else [])
        b3_r1_prev = ("ss", r1)
        rw = [("ss", r1), ("sg", ld_done[t])]
        if res_guard is not None:
            rw.append(res_guard)
        r2 = E("vector", lambda e, par=par: e.tensor_tensor(
            res32[:], resB[:], x2st[:, par * 1024:(par + 1) * 1024], ALU.add),
            waits=rw, inc="vv")
        x2st_vec_free[par] = ("vv", r2)
        b3_r2_prev = ("vv", r2)
        if t + 2 < NT:
            emit_x2_load(t + 2)
        out_done = E("sync", lambda e, t=t: e.dma_start(
            out=outp[t * 128:(t + 1) * 128, :], in_=res32[:]),
            waits=[("vv", r2)], inc="sd")
        res_guard = ("sd", out_done)

    final_sd = W.cnt["sd"]
    final_sg = W.cnt["sg"]
    E("sync", lambda e: e.nop(), waits=[("sd", final_sd)])
    E("gpsimd", lambda e: e.nop(), waits=[("sg", final_sg)])

    with nc.Block() as block:
        @block.sync
        def _(e):
            for f in W.ops["sync"]:
                f(e, sems)

        @block.gpsimd
        def _(e):
            for f in W.ops["gpsimd"]:
                f(e, sems)

        @block.tensor
        def _(e):
            for f in W.ops["tensor"]:
                f(e, sems)

        @block.vector
        def _(e):
            for f in W.ops["vector"]:
                f(e, sems)

        @block.scalar
        def _(e):
            for f in W.ops["scalar"]:
                f(e, sems)

    stk.close()
    return nc


_NC_CACHE = None


def kernel(L0_attn_x, L0_attn_W_K, L0_attn_W_Q, L0_attn_W_V, L0_attn_W_O,
           L0_ffn_weight, L0_ffn_ln_gamma, L0_ffn_ln_beta):
    global _NC_CACHE
    x = np.ascontiguousarray(np.asarray(L0_attn_x, np.float32))
    WK = np.asarray(L0_attn_W_K, np.float32).reshape(1024, 1024)
    WQ = np.asarray(L0_attn_W_Q, np.float32).reshape(1024, 1024)
    WV = np.asarray(L0_attn_W_V, np.float32).reshape(1024, 1024)
    WO = np.asarray(L0_attn_W_O, np.float32)
    WF = np.ascontiguousarray(np.asarray(L0_ffn_weight, np.float32))
    g = np.asarray(L0_ffn_ln_gamma, np.float32)
    bta = np.asarray(L0_ffn_ln_beta, np.float32)

    wkT = np.ascontiguousarray(WK.T)
    wqT = np.ascontiguousarray(WQ.T)
    wvT = np.ascontiguousarray(WV.T)
    woT = np.ascontiguousarray(WO.T)
    gbc = np.ascontiguousarray(np.concatenate(
        [g.reshape(8, 128).T, bta.reshape(8, 128).T], axis=1).astype(np.float32))
    identb = np.eye(128, dtype=ml_dtypes.bfloat16)
    onesf = np.ones((128, 128), np.float32)
    col = np.arange(128)[None, :]
    row = np.arange(128)[:, None]
    maskb = np.where(col > row, np.float32(-1e10),
                     np.float32(0.0)).astype(ml_dtypes.bfloat16)
    cstd = np.ascontiguousarray(np.broadcast_to(
        np.array([0.125, -SHIFT, 1e-5, 1e-3, 1.0, 0.0, -0.5, 0.0], np.float32),
        (128, 8)))
    zrow = np.zeros((1, 512), ml_dtypes.bfloat16)

    if _NC_CACHE is None:
        _NC_CACHE = build_program()
    nc = _NC_CACHE

    shared = dict(wkT=wkT, wqT=wqT, wvT=wvT, woT=woT, wffn=WF, gbc=gbc,
                  identb=identb, onesf=onesf,
                  maskb=maskb, cstd=cstd, zrow=zrow)
    in_maps = []
    for c in range(N_CORES):
        xs = np.ascontiguousarray(x[c * 2:(c + 1) * 2].reshape(NT * 128, D))
        in_maps.append(dict(xin=xs, **shared))

    res = run_bass_kernel_spmd(nc, in_maps, core_ids=list(range(N_CORES)))
    out = np.concatenate([np.asarray(r["out"], np.float32).reshape(2, SEQ, D)
                          for r in res.results], axis=0)
    return out


# version 60 (final)
# speedup vs baseline: 1.0532x; 1.0532x over previous
import contextlib
import numpy as np
import ml_dtypes
import concourse.bass as bass
import concourse.mybir as mybir
from concourse.bass_utils import run_bass_kernel_spmd

F32 = mybir.dt.float32
BF16 = mybir.dt.bfloat16
AF = mybir.ActivationFunctionType
ALU = mybir.AluOpType
AX = mybir.AxisListType

N_CORES = 8
B_PER_CORE = 2
SEQ = 2048
D = 1024
NT = B_PER_CORE * SEQ // 128
NTB = SEQ // 128
SHIFT = 12.0
CLIPF = float(np.float32((128.0 - 1e-5) / 128.0))

ENGS = ("sync", "gpsimd", "tensor", "vector", "scalar")


class Walk:

    def __init__(self):
        self.ops = {e: [] for e in ENGS}
        self.cnt = {"sd": 0, "sg": 0, "tt": 0, "vv": 0, "ss": 0, "sa": 0}
        self.lastw = {}

    def emit(self, eng, build, waits=(), inc=None, inc_amt=None):
        wl = []
        if eng == "vector" and self.cnt["vv"] > 0:
            wl.append(("vv", self.cnt["vv"]))
        for s, v in waits:
            if v is None:
                continue
            assert v <= self.cnt[s], f"wait ahead of emission: {s} {v} > {self.cnt[s]}"
            k = (eng, s)
            if v > self.lastw.get(k, 0):
                wl.append((s, v))
                self.lastw[k] = v
        ret = None
        if inc is not None:
            amt = inc_amt if inc_amt is not None else (16 if inc in ("sd", "sg", "sa") else 1)
            self.cnt[inc] += amt
            ret = self.cnt[inc]
        else:
            amt = 0

        def f(e, sems, wl=tuple(wl), build=build, inc=inc, amt=amt):
            for s, v in wl:
                e.wait_ge(sems[s], v)
            ins = build(e)
            if inc is not None:
                ins.then_inc(sems[inc], amt)

        self.ops[eng].append(f)
        return ret


def cdiv(a, b):
    return (a + b - 1) // b


def build_program():
    nc = bass.Bass("TRN2", target_bir_lowering=False, debug=False)

    xin = nc.dram_tensor("xin", [NT * 128, D], F32, kind="ExternalInput").ap()
    wkT = nc.dram_tensor("wkT", [D, D], F32, kind="ExternalInput").ap()
    wqT = nc.dram_tensor("wqT", [D, D], F32, kind="ExternalInput").ap()
    wvT = nc.dram_tensor("wvT", [D, D], F32, kind="ExternalInput").ap()
    woT = nc.dram_tensor("woT", [D, D], F32, kind="ExternalInput").ap()
    wffn = nc.dram_tensor("wffn", [D, D], F32, kind="ExternalInput").ap()
    gbc_d = nc.dram_tensor("gbc", [128, 16], F32, kind="ExternalInput").ap()
    wqd = nc.dram_tensor("wqd", [D, D], BF16, kind="Internal")
    identb = nc.dram_tensor("identb", [128, 128], BF16, kind="ExternalInput").ap()
    onesf = nc.dram_tensor("onesf", [128, 128], F32, kind="ExternalInput").ap()
    maskb = nc.dram_tensor("maskb", [128, 128], BF16, kind="ExternalInput").ap()
    cstd = nc.dram_tensor("cstd", [128, 8], F32, kind="ExternalInput").ap()
    zrow = nc.dram_tensor("zrow", [1, 512], BF16, kind="ExternalInput").ap()
    outp = nc.dram_tensor("out", [NT * 128, D], F32, kind="ExternalOutput").ap()
    x2d = nc.dram_tensor("x2d", [NT * 128, D], F32, kind="Internal")
    xnd = nc.dram_tensor("xnd", [NT * 128, D], BF16, kind="Internal")
    xn2d = nc.dram_tensor("xn2d", [NT * 128, D], BF16, kind="Internal")
    ccin = nc.dram_tensor("ccin", [128, 1], F32, kind="Internal")
    ccout = nc.dram_tensor("ccout", [128, 1], F32, kind="Internal")

    cstb = nc.alloc_sbuf_tensor("cstb", [128, 8], F32)
    zrob = nc.alloc_sbuf_tensor("zrob", [1, 512], BF16)
    idb = nc.alloc_sbuf_tensor("idb", [128, 128], BF16)
    onb = nc.alloc_sbuf_tensor("onb", [128, 128], F32)
    mkb = nc.alloc_sbuf_tensor("mkb", [128, 128], BF16)
    gbc = nc.alloc_sbuf_tensor("gbcs", [128, 16], F32)
    sgb = nc.alloc_sbuf_tensor("sgb", [128, 1024], BF16)
    sc = nc.alloc_sbuf_tensor("sc", [128, 16], F32)
    stat = nc.alloc_sbuf_tensor("stat", [128, 112], F32)
    rs = nc.alloc_sbuf_tensor("rs", [128, 48], F32)
    wbf = nc.alloc_sbuf_tensor("wbf", [128, 4 * 8192], BF16)
    bfA = nc.alloc_sbuf_tensor("bfA", [128, 32768], BF16)
    ktq = nc.alloc_sbuf_tensor("ktq", [128, 4096], BF16)
    vpb = nc.alloc_sbuf_tensor("vpb", [128, 256], BF16)
    eeb = nc.alloc_sbuf_tensor("eeb", [128, 8192], BF16)
    xnb = nc.alloc_sbuf_tensor("xnb", [128, 1024], BF16)
    xn2t = nc.alloc_sbuf_tensor("xn2t", [128, 2048], BF16)
    xst = nc.alloc_sbuf_tensor("xst", [128, 2048], F32)
    x2st = nc.alloc_sbuf_tensor("x2st", [128, 2048], F32)
    acc32 = nc.alloc_sbuf_tensor("acc32", [128, 1024], F32)
    res32 = nc.alloc_sbuf_tensor("res32", [128, 1024], F32)
    resB = nc.alloc_sbuf_tensor("resB", [128, 1024], F32)
    wstage = nc.alloc_sbuf_tensor("wstage", [128, 1024], F32)
    grow = nc.alloc_sbuf_tensor("grow", [1, 128], F32)

    psSl = [nc.alloc_psum_tensor(f"psS{i}", [128, 512], F32) for i in range(4)]
    psZT = nc.alloc_psum_tensor("psZT", [128, 2048], F32)

    stk = contextlib.ExitStack()
    sems = {s: stk.enter_context(nc.semaphore(s)) for s in ("sd", "sg", "tt", "vv", "ss", "sa")}

    W = Walk()
    E = W.emit

    def mm(out, lhsT, rhs, start, stop, waits=(), tile_position=None):
        return E("tensor",
                 lambda t, o=out, l=lhsT, r=rhs, st_=start, sp=stop, tp=tile_position:
                 t.matmul(o, l, r, start=st_, stop=sp, tile_position=tp),
                 waits=waits, inc="tt")

    E("sync", lambda e: e.dma_start(out=cstb[:], in_=cstd), inc="sd")
    E("sync", lambda e: e.dma_start(out=onb[:], in_=onesf), inc="sd")
    E("sync", lambda e: e.dma_start(out=idb[:], in_=identb), inc="sd")
    E("sync", lambda e: e.dma_start(out=mkb[:], in_=maskb), inc="sd")
    E("sync", lambda e: e.dma_start(out=zrob[:], in_=zrow), inc="sd")
    E("sync", lambda e: e.dma_start(out=gbc[:], in_=gbc_d), inc="sd")
    consts_done = W.cnt["sd"]

    wsrc = [wkT, wqT, wvT, woT]
    wload = None
    for w in range(4):
        for j in range(8):
            wload = E("gpsimd",
                      lambda e, w=w, j=j: e.dma_start(
                          out=wbf[:, w * 8192 + j * 1024:w * 8192 + (j + 1) * 1024],
                          in_=wsrc[w][j * 128:(j + 1) * 128, :]),
                      inc="sg")

    p1 = {"wst_free": None, "red_prev": None, "k": 0}

    def emit_p1_strip():
        k = p1["k"]
        p1["k"] += 1
        ld = E("gpsimd", lambda e, k=k: e.dma_start(out=wstage[:],
                                                    in_=wffn[k * 128:(k + 1) * 128, :]),
               waits=[p1["wst_free"]] if p1["wst_free"] else [], inc="sg")
        aw = [("sg", ld), ("sd", consts_done)]
        if p1["red_prev"] is not None:
            aw.append(("vv", p1["red_prev"]))
        ab_s = E("scalar", lambda e: e.activation(acc32[:], wstage[:], AF.Abs),
                 waits=aw, inc="ss")
        p1["wst_free"] = ("ss", ab_s)
        p1["red_prev"] = E("vector", lambda e, k=k: e.reduce_sum(
            stat[:, 96 + k:97 + k], acc32[:], axis=AX.X), waits=[("ss", ab_s)],
            inc="vv")

    pp = {}

    def emit_beta_pass2():
        sumabs_v = E("vector", lambda e: e.reduce_sum(sc[:, 0:1], stat[:, 96:104],
                                                      axis=AX.X),
                     waits=[("vv", p1["red_prev"])], inc="vv")
        t1w = [("vv", sumabs_v), ("sd", consts_done)]
        if ps_guard[0] is not None:
            t1w.append(ps_guard[0])
        t1 = mm(psSl[0][0:1, 0:1], onb[:, 0:1], sc[:, 0:1], True, True, waits=t1w)
        v1 = E("vector", lambda e: e.tensor_copy(sc[0:1, 1:2], psSl[0][0:1, 0:1]),
               waits=[("tt", t1)], inc="vv")
        t2 = mm(psSl[0][:, 1:2], onb[0:1, 0:128], sc[0:1, 1:2], True, True,
                waits=[("vv", v1)])
        v2 = E("vector", lambda e: e.tensor_copy(sc[:, 2:3], psSl[0][:, 1:2]),
               waits=[("tt", t2)], inc="vv")
        ps_guard[0] = ("vv", v2)
        beta_v = E("vector", lambda e: e.tensor_scalar(
            sc[:, 3:4], sc[:, 2:3], float(2.0 ** -20), None, ALU.mult),
            waits=[("vv", v2)], inc="vv")
        thr_v = E("vector", lambda e: e.tensor_scalar(
            sc[:, 4:5], sc[:, 3:4], 0.5, 5e-6, ALU.mult, ALU.add),
            waits=[("vv", beta_v)], inc="vv")
        bcp_v = E("vector", lambda e: e.tensor_copy(sc[:, 5:6], sc[:, 3:4]),
                  waits=[("vv", beta_v)], inc="vv")
        bsq_v = E("vector", lambda e: e.tensor_tensor(
            sc[:, 6:7], sc[:, 3:4], sc[:, 5:6], ALU.mult),
            waits=[("vv", bcp_v)], inc="vv")
        pp.update(thr_v=thr_v, bsq_v=bsq_v, wst_free2=p1["wst_free"],
                  mk_prev=None, wq_store=None, k=0)

    def emit_p2_strip():
        k = pp["k"]
        pp["k"] += 1
        ld2 = E("gpsimd", lambda e, k=k: e.dma_start(
            out=wstage[:], in_=wffn[k * 128:(k + 1) * 128, :]),
            waits=[pp["wst_free2"]] if pp["wst_free2"] else [], inc="sg")
        aw = [("sg", ld2)]
        if pp["mk_prev"] is not None:
            aw.append(("vv", pp["mk_prev"]))
        ab2 = E("scalar", lambda e: e.activation(res32[:], wstage[:], AF.Abs),
                waits=aw, inc="ss")
        mkw = [("ss", ab2), ("vv", pp["thr_v"])]
        if pp["mk_prev"] is None and a1b1["st_last"] is not None:
            mkw.append(("sd", a1b1["st_last"]))
        mk = E("vector", lambda e: e.tensor_scalar(
            xnb[:], res32[:], sc[:, 4:5], None, ALU.is_gt),
            waits=mkw, inc="vv")
        pp["mk_prev"] = mk
        sg_ = E("scalar", lambda e: e.activation(sgb[:], wstage[:], AF.Sign),
                waits=[("sg", ld2)], inc="ss")
        pp["wst_free2"] = ("ss", sg_)
        ww = [("vv", mk), ("ss", sg_)]
        if pp["wq_store"] is not None:
            ww.append(pp["wq_store"])
        tt_ = E("vector", lambda e: e.tensor_tensor(
            xn2t[:, 1024:2048], xnb[:], sgb[:], ALU.mult), waits=ww,
            inc="vv")
        pp["wq_store"] = ("sd", E("sync", lambda e, k=k: e.dma_start(
            out=wqd.ap()[k * 128:(k + 1) * 128, :], in_=xn2t[:, 1024:2048]),
            waits=[("vv", tt_)], inc="sd"))

    ps_guard = {0: None, 1: None, 2: None, 3: None}

    prev_sqrt = {}
    prev_recip = {}

    def ln_bn(src0, src1, eps_col, base, w0, w1, mid=None, var_scale=None):
        E("vector", lambda e, base=base: e.bn_stats(
            stat[:, base:base + 6], src0), waits=w0, inc="vv")
        s1 = E("vector", lambda e, base=base: e.bn_stats(
            stat[:, base + 6:base + 12], src1), waits=w1, inc="vv")
        aw = [("vv", s1)]
        if base in prev_sqrt:
            aw.append(("ss", prev_sqrt[base]))
        ag = E("vector", lambda e, base=base: e.bn_aggr(
            stat[:, base + 12:base + 14], stat[:, base:base + 12]),
            waits=aw, inc="vv")
        sw = [("vv", ag), ("sd", consts_done)]
        ln_s = E("scalar", lambda e, base=base: e.activation(
            stat[:, base + 14:base + 15], stat[:, base + 13:base + 14],
            AF.Ln, bias=cstb[:, eps_col:eps_col + 1]), waits=sw, inc="ss")
        prev_sqrt[base] = ln_s
        if mid is not None:
            mid()
        rstd_s = E("scalar", lambda e, base=base: e.activation(
            stat[:, base + 15:base + 16], stat[:, base + 14:base + 15],
            AF.Exp, bias=cstb[:, 5:6], scale=cstb[:, 6:7]),
            waits=[("ss", ln_s)], inc="ss")
        prev_recip[base] = rstd_s
        return ("ss", rstd_s)

    prev_apply = {}

    def ln_apply_scalar(base, src_ap, out_ap, rstd_tok, extra_waits=()):
        mcol = 105 + base // 16
        mw = [rstd_tok]
        if base in prev_apply:
            mw.append(("ss", prev_apply[base]))
        mrs = E("vector", lambda e, base=base, mcol=mcol: e.tensor_scalar(
            stat[:, mcol:mcol + 1], stat[:, base + 12:base + 13], -1.0,
            stat[:, base + 15:base + 16], ALU.mult, ALU.mult),
            waits=mw, inc="vv")
        ap_s = E("scalar", lambda e, base=base, mcol=mcol: e.activation(
            out_ap, src_ap, AF.Identity, bias=stat[:, mcol:mcol + 1],
            scale=stat[:, base + 15:base + 16]),
            waits=[("vv", mrs)] + list(extra_waits), inc="ss")
        prev_apply[base] = ap_s
        return ap_s


    xst_free = [None, None]
    a4_r1_prev = None
    a4_r2_prev = None
    x2st_store_free = [None, None]
    x2st_vec_free = [None, None]
    xn2b_guard = None
    xn2w_last = None
    x2w_done = [None] * NT
    qkv_wk_last = None
    qkv_done_prev = None

    a1b1 = {"lt": 0, "ct": 0, "ld": {}, "xnb_guard": None, "st_last": None}

    def emit_a1b1_load():
        t = a1b1["lt"]
        a1b1["lt"] += 1
        par = t % 2
        gt = NTB + t
        xw = [xst_free[par]] if xst_free[par] else []
        a1b1["ld"][t] = E("sync", lambda e, gt=gt, par=par: e.dma_start(
            out=xst[:, par * 1024:(par + 1) * 1024],
            in_=xin[gt * 128:(gt + 1) * 128, :]), waits=xw, inc="sd")

    def emit_a1b1_compute():
        t = a1b1["ct"]
        a1b1["ct"] += 1
        par = t % 2
        gt = NTB + t
        base = 16 * par
        ld = a1b1["ld"][t]
        src = xst[:, par * 1024:(par + 1) * 1024]
        rstd_v = ln_bn(xst[:, par * 1024:par * 1024 + 512],
                       xst[:, par * 1024 + 512:(par + 1) * 1024],
                       2, base, [("sd", ld)], [("sd", ld)])
        xn_s = ln_apply_scalar(base, src, xnb[:], rstd_v,
                               [a1b1["xnb_guard"]] if a1b1["xnb_guard"] else [])
        xst_free[par] = ("ss", xn_s)
        st_ = E("sync", lambda e, gt=gt: e.dma_start(
            out=xnd.ap()[gt * 128:(gt + 1) * 128, :], in_=xnb[:]),
            waits=[("ss", xn_s)], inc="sd")
        a1b1["xnb_guard"] = ("sd", st_)
        a1b1["st_last"] = st_

    for b in range(B_PER_CORE):
        if b == 1:
            emit_beta_pass2()
        xnb_guard = None
        a1_ld_pair = None
        for t in (range(NTB) if b == 0 else []):
            gt = b * NTB + t
            par = t % 2
            base = 16 * par
            if t % 2 == 0:
                xw = [w for w in (xst_free[0], xst_free[1]) if w]
                a1_ld_pair = E("sync", lambda e, gt=gt: e.dma_start(
                    out=xst[:],
                    in_=xin[gt * 128:(gt + 2) * 128, :].rearrange(
                        "(t p) c -> p t c", t=2)), waits=xw, inc="sd")
            ld = a1_ld_pair
            src = xst[:, par * 1024:(par + 1) * 1024]
            src0 = xst[:, par * 1024:par * 1024 + 512]
            src1 = xst[:, par * 1024 + 512:(par + 1) * 1024]
            rstd_v = ln_bn(src0, src1, 2, base, [("sd", ld)], [("sd", ld)])
            xn_s = ln_apply_scalar(base, src, xnb[:], rstd_v,
                                   [xnb_guard] if xnb_guard else [])
            xst_free[par] = ("ss", xn_s)
            st_ = E("sync", lambda e, gt=gt: e.dma_start(
                out=xnd.ap()[gt * 128:(gt + 1) * 128, :], in_=xnb[:]),
                waits=[("ss", xn_s)], inc="sd")
            xnb_guard = ("sd", st_)
        trl = None
        for j in range(8):
            trw = [xnb_guard if b == 0 else ("sd", a1b1["st_last"])]
            if b > 0 and qkv_done_prev is not None:
                trw.append(("tt", qkv_done_prev))
            trl = E("sync", lambda e, j=j, b=b: e.dma_start(
                out=bfA[:, j * 2048:(j + 1) * 2048],
                in_=xnd.ap()[b * SEQ:(b + 1) * SEQ, j * 128:(j + 1) * 128],
                transpose=True), waits=trw, inc="sd")
        xnT_ready = ("sd", trl)
        if b == 0:
            emit_a1b1_load()
            emit_a1b1_load()

        zT_evac_v = None
        ktq_guard = None
        for p in range(8):
            kt_last_evac = None
            rot = 0
            for which, woff, koff in ((0, 0, 0), (1, 8192, 2048)):
                for cb in range(4):
                    psi = rot % 4
                    rot += 1
                    ps = psSl[psi]
                    gw = [("sg", wload), xnT_ready]
                    if ktq_guard is not None:
                        gw.append(ktq_guard)
                    if ps_guard[psi] is not None:
                        gw.append(ps_guard[psi])
                    last = None
                    col0 = cb * 512
                    for j in range(8):
                        wts = gw if j == 0 else ()
                        last = mm(ps[:],
                                  wbf[:, woff + j * 1024 + p * 128:
                                      woff + j * 1024 + (p + 1) * 128],
                                  bfA[:, j * 2048 + col0: j * 2048 + col0 + 512],
                                  j == 0, j == 7, waits=wts)
                    ev = E("vector", lambda e, ps=ps, koff=koff, cb=cb: e.tensor_copy(
                        ktq[:, koff + cb * 512: koff + (cb + 1) * 512], ps[:]),
                        waits=[("tt", last)], inc="vv")
                    ps_guard[psi] = ("vv", ev)
                    kt_last_evac = ev
            if p == 7:
                qkv_wk_last = W.cnt["tt"]

            kq_ready = ("vv", kt_last_evac)
            zt_anti = ("vv", zT_evac_v) if zT_evac_v is not None else None
            exp_done = {}
            z_tok = {}
            vp_tok = {}
            rs_guard = {}
            zero_done = {}
            last_stop_mm = None

            def emit_z(i):
                nonlocal last_stop_mm
                par = i % 2
                ncols = (i + 1) * 128
                for q in range(cdiv(ncols, 512)):
                    wq_ = min(512, ncols - q * 512)
                    if q not in zero_done:
                        zw = [("vv", vp_tok[i])]
                        if zt_anti is not None:
                            zw.append(zt_anti)
                        zero_done[q] = mm(psZT[:, q * 512:q * 512 + wq_],
                                          zrob[0:1, 0:128], zrob[0:1, 0:wq_],
                                          True, False, waits=zw)
                    for h in range(2):
                        stop = (i == NTB - 1) and (h == 1)
                        zw = [("vv", vp_tok[i]), ("ss", exp_done[(h, i, q)])]
                        last_stop_mm = mm(
                            psZT[h * 64:(h + 1) * 64, q * 512:q * 512 + wq_],
                            vpb[:, par * 128 + h * 64: par * 128 + (h + 1) * 64],
                            eeb[:, par * 4096 + h * 2048 + q * 512:
                                par * 4096 + h * 2048 + q * 512 + wq_],
                            False, stop, waits=zw,
                            tile_position=(0, h * 64))
                z_tok[i] = last_stop_mm

            rot2 = 0
            for i in range(NTB):
                par = i % 2
                ncols = (i + 1) * 128
                nb5 = cdiv(ncols, 512)
                for h in range(2):
                    prow = slice(h * 64, h * 64 + 64)
                    for b5 in range(nb5):
                        psi = rot2 % 4
                        rot2 += 1
                        pS = psSl[psi]
                        wblk = min(512, ncols - b5 * 512)
                        wts = [kq_ready]
                        if ps_guard[psi] is not None:
                            wts.append(ps_guard[psi])
                        diag = b5 == nb5 - 1
                        last = mm(pS[:, 0:wblk],
                                  ktq[prow, i * 128:(i + 1) * 128],
                                  ktq[prow, 2048 + b5 * 512:
                                      2048 + b5 * 512 + wblk],
                                  True, not diag, waits=wts)
                        if diag:
                            last = mm(pS[:, wblk - 128:wblk],
                                      idb[:], mkb[:], False, True,
                                      waits=[("sd", consts_done)])
                        acol = par * 24 + h * 4 + b5
                        eoff = par * 4096 + h * 2048 + b5 * 512
                        ew = [("tt", last)]
                        if (i - 2) in z_tok:
                            ew.append(("tt", z_tok[i - 2]))
                        if rs_guard.get((par, h)) is not None:
                            ew.append(("vv", rs_guard[(par, h)]))
                        tok = E("scalar",
                                lambda e, pS=pS, wblk=wblk, eoff=eoff, acol=acol:
                                e.activation(
                                    eeb[:, eoff:eoff + wblk], pS[:, 0:wblk], AF.Exp,
                                    bias=cstb[:, 1:2], scale=cstb[:, 0:1],
                                    accum_out=rs[:, acol:acol + 1]),
                                waits=ew, inc="ss")
                        exp_done[(h, i, b5)] = tok
                        ps_guard[psi] = ("ss", tok)
                psi = rot2 % 4
                rot2 += 1
                psV = psSl[psi]
                gv = [("sg", wload), xnT_ready]
                if ps_guard[psi] is not None:
                    gv.append(ps_guard[psi])
                lastv = None
                for j in range(8):
                    wts = gv if j == 0 else ()
                    lastv = mm(psV[:, 0:128],
                               bfA[:, j * 2048 + i * 128: j * 2048 + (i + 1) * 128],
                               wbf[:, 2 * 8192 + j * 1024 + p * 128:
                                   2 * 8192 + j * 1024 + (p + 1) * 128],
                               j == 0, j == 7, waits=wts)
                vp_v = None
                for h in range(2):
                    pc = par * 24
                    if nb5 == 1:
                        tot = E("vector", lambda e, h=h, pc=pc: e.reciprocal(
                            rs[:, pc + 12 + h:pc + 13 + h],
                            rs[:, pc + h * 4:pc + h * 4 + 1]),
                            waits=[("ss", exp_done[(h, i, 0)])], inc="vv")
                    else:
                        ca, cb_ = pc + 8 + h, pc + 10 + h
                        s_ = E("vector", lambda e, h=h, pc=pc, ca=ca: e.tensor_add(
                            rs[:, ca:ca + 1], rs[:, pc + h * 4:pc + h * 4 + 1],
                            rs[:, pc + h * 4 + 1:pc + h * 4 + 2]),
                            waits=[("ss", exp_done[(h, i, 1)])], inc="vv")
                        cur = ca
                        for b5 in range(2, nb5):
                            nxt = cb_ if cur == ca else ca
                            s_ = E("vector",
                                   lambda e, h=h, pc=pc, b5=b5, cur=cur, nxt=nxt:
                                   e.tensor_add(
                                       rs[:, nxt:nxt + 1], rs[:, cur:cur + 1],
                                       rs[:, pc + h * 4 + b5:pc + h * 4 + b5 + 1]),
                                   waits=[("ss", exp_done[(h, i, b5)])], inc="vv")
                            cur = nxt
                        tot = E("vector", lambda e, h=h, pc=pc, cur=cur: e.reciprocal(
                            rs[:, pc + 12 + h:pc + 13 + h], rs[:, cur:cur + 1]),
                            waits=[("vv", s_)], inc="vv")
                    rs_guard[(par, h)] = tot
                    vw2 = [("vv", tot), ("tt", lastv)]
                    if (i - 2) in z_tok:
                        vw2.append(("tt", z_tok[i - 2]))
                    vp_v = E("vector", lambda e, h=h, par=par, pc=pc, psV=psV:
                             e.tensor_scalar(
                                 vpb[:, par * 128 + h * 64: par * 128 + (h + 1) * 64],
                                 psV[:, h * 64:(h + 1) * 64],
                                 rs[:, pc + 12 + h:pc + 13 + h], None, ALU.mult),
                             waits=vw2, inc="vv")
                vp_tok[i] = vp_v
                ps_guard[psi] = ("vv", vp_v)
                if i > 0:
                    emit_z(i - 1)
            emit_z(NTB - 1)
            zT_evac_v = E("vector", lambda e, p=p: e.tensor_copy(
                bfA[:, 16384 + p * 2048: 16384 + (p + 1) * 2048], psZT[:]),
                waits=[("tt", last_stop_mm)], inc="vv")
            ktq_guard = ("tt", last_stop_mm)
            if b == 0:
                emit_p1_strip()
                emit_a1b1_compute()
                emit_a1b1_compute()
                if p < 7:
                    emit_a1b1_load()
                    emit_a1b1_load()
            else:
                emit_p2_strip()
        qkv_done_prev = W.cnt["tt"]

        a4_ld = {}

        def a4_load(t):
            par = t % 2
            gt_ = b * NTB + t
            xw = [xst_free[par]] if xst_free[par] else []
            a4_ld[t] = E("gpsimd", lambda e, gt_=gt_, par=par: e.dma_start(
                out=xst[:, par * 1024:(par + 1) * 1024],
                in_=xin[gt_ * 128:(gt_ + 1) * 128, :]), waits=xw, inc="sg")

        a4_mm = {}

        def a4_mms_evacs(t):
            nonlocal a4_r1_prev
            lasts = []
            for eb in range(2):
                psi = (2 * t + eb) % 4
                ps = psSl[psi]
                gw = [("vv", zT_evac_v), ("sg", wload)]
                if ps_guard[psi] is not None:
                    gw.append(ps_guard[psi])
                last = None
                for j in range(8):
                    wts = gw if j == 0 else ()
                    last = mm(ps[:],
                              bfA[:, 16384 + j * 2048 + t * 128:
                                  16384 + j * 2048 + (t + 1) * 128],
                              wbf[:, 3 * 8192 + j * 1024 + eb * 512:
                                  3 * 8192 + j * 1024 + (eb + 1) * 512],
                              j == 0, j == 7, waits=wts)
                lasts.append((last, psi, ps))
            evw = [("tt", lasts[0][0])]
            if a4_r1_prev is not None:
                evw.append(a4_r1_prev)
            ev0 = E("scalar", lambda e, ps=lasts[0][2]: e.activation(
                acc32[:, 0:512], ps[:], AF.Copy), waits=evw, inc="ss")
            ev = E("scalar", lambda e, ps=lasts[1][2]: e.activation(
                acc32[:, 512:1024], ps[:], AF.Copy),
                waits=[("tt", lasts[1][0])], inc="ss")
            ps_guard[lasts[0][1]] = ("ss", ev0)
            ps_guard[lasts[1][1]] = ("ss", ev)
            a4_mm[t] = (ev0, ev)

        a4_load(0)
        a4_load(1)
        a4_mms_evacs(0)
        for t in range(NTB):
            gt = b * NTB + t
            par = t % 2
            base = 32 + 16 * par
            base2 = 64 + 16 * par
            ev0, ev = a4_mm[t]
            rstd_v = ln_bn(acc32[:, 0:512], acc32[:, 512:1024], 2, base,
                           [("ss", ev0)], [("ss", ev)])
            r1 = ln_apply_scalar(base, acc32[:], resB[:], rstd_v,
                                 [a4_r2_prev] if a4_r2_prev else [])
            a4_r1_prev = ("ss", r1)
            r2w = [("ss", r1), ("sg", a4_ld[t])]
            if x2st_store_free[par] is not None:
                r2w.append(x2st_store_free[par])
            if x2st_vec_free[par] is not None:
                r2w.append(x2st_vec_free[par])
            r2 = E("vector", lambda e, par=par: e.tensor_tensor(
                x2st[:, par * 1024:(par + 1) * 1024], resB[:],
                xst[:, par * 1024:(par + 1) * 1024], ALU.add),
                waits=r2w, inc="vv")
            xst_free[par] = ("vv", r2)
            a4_r2_prev = ("vv", r2)
            if t + 2 < NTB:
                a4_load(t + 2)
            if t + 1 < NTB:
                a4_mms_evacs(t + 1)
            st_eng, st_sem = ("sync", "sd") if b == 0 else ("gpsimd", "sg")
            x2w_done[gt] = (st_sem, E(st_eng, lambda e, gt=gt, par=par: e.dma_start(
                out=x2d.ap()[gt * 128:(gt + 1) * 128, :],
                in_=x2st[:, par * 1024:(par + 1) * 1024]),
                waits=[("vv", r2)], inc=st_sem))
            x2st_store_free[par] = x2w_done[gt]
            rstd_b1 = ln_bn(x2st[:, par * 1024:par * 1024 + 512],
                            x2st[:, par * 1024 + 512:(par + 1) * 1024],
                            3, base2, [("vv", r2)], [("vv", r2)])
            xc = ln_apply_scalar(base2, x2st[:, par * 1024:(par + 1) * 1024],
                                 xn2t[:, 0:1024], rstd_b1,
                                 [xn2b_guard] if xn2b_guard else [])
            x2st_vec_free[par] = ("ss", xc)
            xn2w_last = E("gpsimd", lambda e, gt=gt: e.dma_start(
                out=xn2d.ap()[gt * 128:(gt + 1) * 128, :], in_=xn2t[:, 0:1024]),
                waits=[("ss", xc)], inc="sg")
            xn2b_guard = ("sg", xn2w_last)

        if b == 0:
            xn2w_b0_last = xn2w_last

    oproj_tt = W.cnt["tt"]

    wq_ready = None
    for k in range(8):
        wq_ready = E("gpsimd", lambda e, k=k: e.dma_start(
            out=wbf[:, k * 1024:(k + 1) * 1024],
            in_=wqd.ap()[k * 128:(k + 1) * 128, :]),
            waits=[pp["wq_store"], ("tt", qkv_wk_last)], inc="sg")
    wq_ready = ("sg", wq_ready)

    STG = 8192
    u_free = {}
    gmax_v = None
    gcnt = 0
    for idx, (h, j) in enumerate([(h, j) for h in range(2) for j in range(8)]):
        slot = idx % 4
        tw = [("tt", qkv_wk_last)]
        if h == 0:
            tw.append(("sg", xn2w_b0_last))
        else:
            tw.append(("sg", xn2w_last))
        if slot in u_free:
            tw.append(u_free[slot])
        trl = E("sync", lambda e, j=j, h=h, slot=slot: e.dma_start(
            out=wbf[:, STG + slot * 2048:STG + (slot + 1) * 2048],
            in_=xn2d.ap()[h * 2048:(h + 1) * 2048, j * 128:(j + 1) * 128],
            transpose=True), waits=tw, inc="sd")
        dst = j * 4096 + h * 2048
        uw = [("sd", trl)]
        if dst < 16384:
            uw.append(("tt", qkv_done_prev))
        else:
            uw.append(("tt", oproj_tt))
        u = E("vector", lambda e, j=j, slot=slot, dst=dst: e.tensor_scalar(
            bfA[:, dst:dst + 2048], wbf[:, STG + slot * 2048:STG + (slot + 1) * 2048],
            gbc[:, j:j + 1], gbc[:, 8 + j:9 + j], ALU.mult, ALU.add),
            waits=uw, inc="vv")
        u_free[slot] = ("vv", u)
        mxw = E("vector", lambda e, dst=dst: e.reduce_max(
            stat[:, 104:105], bfA[:, dst:dst + 2048], axis=AX.X,
            apply_absolute_value=True), waits=[("vv", u)], inc="vv")
        grd = 8 + ((gcnt - 1) % 2)
        gwr = 8 + (gcnt % 2)
        if gcnt == 0:
            gmax_v = E("vector", lambda e: e.tensor_copy(sc[:, 8:9],
                                                         stat[:, 104:105]),
                       waits=[("vv", mxw)], inc="vv")
        else:
            gmax_v = E("vector", lambda e, grd=grd, gwr=gwr: e.tensor_tensor(
                sc[:, gwr:gwr + 1], sc[:, grd:grd + 1], stat[:, 104:105],
                ALU.max), waits=[("vv", mxw)], inc="vv")
        gcnt += 1
    apply_done = ("vv", gmax_v)

    gcol = 8 + ((32 - 1) % 2)
    g1 = E("gpsimd", lambda e, gcol=gcol: e.dma_start(out=ccin.ap(),
                                                      in_=sc[:, gcol:gcol + 1]),
           waits=[("vv", gmax_v)], inc="sg")
    g2 = E("gpsimd", lambda e: e.collective_compute(
        "AllReduce", ALU.max, replica_groups=[list(range(N_CORES))],
        ins=[ccin.ap()], outs=[ccout.ap()]), waits=[("sg", g1)], inc="sg", inc_amt=1)
    g3 = E("gpsimd", lambda e: e.dma_start(out=grow[:],
                                           in_=ccout.ap().rearrange("p o -> o p")),
           waits=[("sg", g2)], inc="sg")
    vg = E("vector", lambda e: e.reduce_max(sc[0:1, 10:11], grow[0:1, :], axis=AX.X),
           waits=[("sg", g3)], inc="vv")
    tbw = [("vv", vg)]
    if ps_guard[0] is not None:
        tbw.append(ps_guard[0])
    tb = mm(psSl[0][:, 1:2], onb[0:1, 0:128], sc[0:1, 10:11], True, True, waits=tbw)
    vb = E("vector", lambda e: e.tensor_copy(sc[:, 11:12], psSl[0][:, 1:2]),
           waits=[("tt", tb)], inc="vv")
    ps_guard[0] = ("vv", vb)
    E("vector", lambda e: e.tensor_scalar(sc[:, 12:13], sc[:, 11:12], CLIPF, None,
                                          ALU.mult), waits=[("vv", vb)], inc="vv")
    tclip_v = E("vector", lambda e: e.tensor_scalar(sc[:, 13:14], sc[:, 11:12], -CLIPF,
                                                    None, ALU.mult), inc="vv")

    res_guard = None
    out_done = None
    b3_r1_prev = None
    b3_r2_prev = None
    xn2t_guard = [None, None]
    clip_done = {}
    ld_done = {}

    bfA_j = bfA[:, 0:32768].rearrange("p (j c) -> p j c", j=8)

    def emit_clips(t):
        par = t % 2
        cw = [("vv", tclip_v), apply_done]
        if xn2t_guard[par] is not None:
            cw.append(xn2t_guard[par])
        xt = E("vector", lambda e, t=t, par=par: e.tensor_scalar(
            xn2t[:, par * 1024:(par + 1) * 1024],
            bfA_j[:, :, t * 128:(t + 1) * 128],
            sc[:, 12:13], sc[:, 13:14],
            ALU.min, ALU.max), waits=cw, inc="vv")
        clip_done[t] = ("vv", xt)

    def emit_x2_load(t):
        par = t % 2
        lw = [x2w_done[t]]
        if x2st_store_free[par] is not None:
            lw.append(x2st_store_free[par])
        if x2st_vec_free[par] is not None:
            lw.append(x2st_vec_free[par])
        ld_done[t] = E("gpsimd", lambda e, t=t, par=par: e.dma_start(
            out=x2st[:, par * 1024:(par + 1) * 1024],
            in_=x2d.ap()[t * 128:(t + 1) * 128, :]), waits=lw, inc="sg")

    emit_clips(0)
    emit_clips(1)
    emit_x2_load(0)
    emit_x2_load(1)
    ps_guard2 = {0: None, 1: None, 2: None, 3: None}
    for t in range(NT):
        par = t % 2
        base = 32 + 16 * par
        lasts = []
        for ub in range(2):
            psi = (2 * t + ub) % 4
            ps = psSl[psi]
            gw = [clip_done[t], wq_ready]
            if ps_guard[psi] is not None:
                gw.append(ps_guard[psi])
            if ps_guard2[psi] is not None:
                gw.append(ps_guard2[psi])
            last = None
            for j in range(8):
                wts = gw if j == 0 else ()
                last = mm(ps[:],
                          xn2t[:, par * 1024 + j * 128: par * 1024 + (j + 1) * 128],
                          wbf[:, j * 1024 + ub * 512: j * 1024 + (ub + 1) * 512],
                          j == 0, j == 7, waits=wts)
            lasts.append((last, psi, ps))
        xn2t_guard[par] = ("tt", lasts[1][0])
        yw0 = [("tt", lasts[0][0]), ("vv", pp["thr_v"])]
        if b3_r1_prev is not None:
            yw0.append(b3_r1_prev)
        yv0 = E("scalar", lambda e, ps=lasts[0][2]: e.activation(
            acc32[:, 0:512], ps[:], AF.Copy, scale=sc[:, 3:4]), waits=yw0, inc="ss")
        yv = E("scalar", lambda e, ps=lasts[1][2]: e.activation(
            acc32[:, 512:1024], ps[:], AF.Copy, scale=sc[:, 3:4]),
            waits=[("tt", lasts[1][0])], inc="ss")
        ps_guard[lasts[0][1]] = ("ss", yv0)
        ps_guard[lasts[1][1]] = ("ss", yv)
        rstd_v = ln_bn(acc32[:, 0:512], acc32[:, 512:1024], 2, base,
                       [("ss", yv0)], [("ss", yv)],
                       mid=(lambda t=t: emit_clips(t + 2)) if t + 2 < NT else None)
        r1 = ln_apply_scalar(base, acc32[:], resB[:], rstd_v,
                             [b3_r2_prev] if b3_r2_prev else [])
        b3_r1_prev = ("ss", r1)
        rw = [("ss", r1), ("sg", ld_done[t])]
        if res_guard is not None:
            rw.append(res_guard)
        r2 = E("vector", lambda e, par=par: e.tensor_tensor(
            res32[:], resB[:], x2st[:, par * 1024:(par + 1) * 1024], ALU.add),
            waits=rw, inc="vv")
        x2st_vec_free[par] = ("vv", r2)
        b3_r2_prev = ("vv", r2)
        if t + 2 < NT:
            emit_x2_load(t + 2)
        out_done = E("sync", lambda e, t=t: e.dma_start(
            out=outp[t * 128:(t + 1) * 128, :], in_=res32[:]),
            waits=[("vv", r2)], inc="sd")
        res_guard = ("sd", out_done)

    final_sd = W.cnt["sd"]
    final_sg = W.cnt["sg"]
    E("sync", lambda e: e.nop(), waits=[("sd", final_sd)])
    E("gpsimd", lambda e: e.nop(), waits=[("sg", final_sg)])

    with nc.Block() as block:
        @block.sync
        def _(e):
            for f in W.ops["sync"]:
                f(e, sems)

        @block.gpsimd
        def _(e):
            for f in W.ops["gpsimd"]:
                f(e, sems)

        @block.tensor
        def _(e):
            for f in W.ops["tensor"]:
                f(e, sems)

        @block.vector
        def _(e):
            for f in W.ops["vector"]:
                f(e, sems)

        @block.scalar
        def _(e):
            for f in W.ops["scalar"]:
                f(e, sems)

    stk.close()
    return nc


_NC_CACHE = None


def kernel(L0_attn_x, L0_attn_W_K, L0_attn_W_Q, L0_attn_W_V, L0_attn_W_O,
           L0_ffn_weight, L0_ffn_ln_gamma, L0_ffn_ln_beta):
    global _NC_CACHE
    x = np.ascontiguousarray(np.asarray(L0_attn_x, np.float32))
    WK = np.asarray(L0_attn_W_K, np.float32).reshape(1024, 1024)
    WQ = np.asarray(L0_attn_W_Q, np.float32).reshape(1024, 1024)
    WV = np.asarray(L0_attn_W_V, np.float32).reshape(1024, 1024)
    WO = np.asarray(L0_attn_W_O, np.float32)
    WF = np.ascontiguousarray(np.asarray(L0_ffn_weight, np.float32))
    g = np.asarray(L0_ffn_ln_gamma, np.float32)
    bta = np.asarray(L0_ffn_ln_beta, np.float32)

    wkT = np.ascontiguousarray(WK.T)
    wqT = np.ascontiguousarray(WQ.T)
    wvT = np.ascontiguousarray(WV.T)
    woT = np.ascontiguousarray(WO.T)
    gbc = np.ascontiguousarray(np.concatenate(
        [g.reshape(8, 128).T, bta.reshape(8, 128).T], axis=1).astype(np.float32))
    identb = np.eye(128, dtype=ml_dtypes.bfloat16)
    onesf = np.ones((128, 128), np.float32)
    col = np.arange(128)[None, :]
    row = np.arange(128)[:, None]
    maskb = np.where(col > row, np.float32(-1e10),
                     np.float32(0.0)).astype(ml_dtypes.bfloat16)
    cstd = np.ascontiguousarray(np.broadcast_to(
        np.array([0.125, -SHIFT, 1e-5, 1e-3, 1.0, 0.0, -0.5, 0.0], np.float32),
        (128, 8)))
    zrow = np.zeros((1, 512), ml_dtypes.bfloat16)

    if _NC_CACHE is None:
        _NC_CACHE = build_program()
    nc = _NC_CACHE

    shared = dict(wkT=wkT, wqT=wqT, wvT=wvT, woT=woT, wffn=WF, gbc=gbc,
                  identb=identb, onesf=onesf,
                  maskb=maskb, cstd=cstd, zrow=zrow)
    in_maps = []
    for c in range(N_CORES):
        xs = np.ascontiguousarray(x[c * 2:(c + 1) * 2].reshape(NT * 128, D))
        in_maps.append(dict(xin=xs, **shared))

    res = run_bass_kernel_spmd(nc, in_maps, core_ids=list(range(N_CORES)))
    out = np.concatenate([np.asarray(r["out"], np.float32).reshape(2, SEQ, D)
                          for r in res.results], axis=0)
    return out
